# version 24
# baseline (speedup 1.0000x reference)
import contextlib
import numpy as np

import concourse.bass as bass
import concourse.mybir as mybir
from concourse.bass_utils import run_bass_kernel_spmd

F32 = mybir.dt.float32
BF16 = mybir.dt.bfloat16
ALU = mybir.AluOpType
AF = mybir.ActivationFunctionType
AX = mybir.AxisListType

NCORES = 8
NSEQ = 2
S = 2048
C = 1024
CH = 8
TT = 512
NTT = S // TT
H = 8
RK = 64
NKT = S // 128
QG = 512
NQG = S // QG
XNC = S + 1
EXP_BIAS = -20.0

ENGS = ("tensor", "vector", "scalar", "gpsimd", "sync")
DEBUG = False


class Sched:

    def __init__(self, nc, es):
        self.nc = nc
        self.ops = {e: [] for e in ENGS}
        self.msem = {e: es.enter_context(nc.semaphore(f"ms_{e}")) for e in ENGS}
        self.dsem = {e: es.enter_context(nc.semaphore(f"ds_{e}")) for e in ("gpsimd", "sync", "scalar")}
        self.mcnt = {e: 0 for e in ENGS}
        self.dcnt = {e: 0 for e in self.dsem}
        self.waited = {e: {} for e in ENGS}
        self.pending = {e: [] for e in ENGS}

    def _resolve(self, eng, waits):
        out = []
        for tok in waits:
            if tok is None:
                continue
            kind, weng, val = tok
            if kind == "m" and weng == eng:
                continue
            key = (kind, weng)
            if self.waited[eng].get(key, 0) >= val:
                continue
            self.waited[eng][key] = val
            sem = self.msem[weng] if kind == "m" else self.dsem[weng]
            out.append((sem, val))
        return out

    def emit(self, eng, fn, waits=(), inc=False, dma=False):
        allw = list(self.pending[eng]) + list(waits)
        self.pending[eng] = []
        wl = self._resolve(eng, allw)
        tok = None
        sem = None
        amt = 0
        if dma:
            self.dcnt[eng] += 16
            tok = ("d", eng, self.dcnt[eng])
            sem, amt = self.dsem[eng], 16
        elif inc:
            self.mcnt[eng] += 1
            tok = ("m", eng, self.mcnt[eng])
            sem, amt = self.msem[eng], 1

        def clos(e, fn=fn, wl=wl, sem=sem, amt=amt):
            for s_, v_ in wl:
                e.wait_ge(s_, v_)
            ins = fn(e)
            if sem is not None:
                ins.then_inc(sem, amt)

        self.ops[eng].append(clos)
        return tok

    def flush_dma(self, eng):
        if self.dcnt.get(eng, 0) == 0:
            return None
        tok = ("d", eng, self.dcnt[eng])
        return self.emit(eng, lambda e: e.nop(), waits=[tok], inc=True)

    def barrier(self):
        toks = [("m", e, self.mcnt[e]) for e in ENGS if self.mcnt[e] > 0]
        toks += [("d", e, self.dcnt[e]) for e in self.dcnt if self.dcnt[e] > 0]
        for e in ENGS:
            self.pending[e].extend(toks)


def build_nc():
    nc = bass.Bass("TRN2", target_bir_lowering=False)

    x_in = nc.dram_tensor("x", [NSEQ * S, C], F32, kind="ExternalInput")
    wq_d = nc.dram_tensor("wq", [2048, 1024], F32, kind="ExternalInput")
    wkup_d = nc.dram_tensor("wkup", [2048, 64], F32, kind="ExternalInput")
    wvup_d = nc.dram_tensor("wvup", [2048, 64], F32, kind="ExternalInput")
    wot_d = nc.dram_tensor("wot", [2048, 1024], F32, kind="ExternalInput")
    wkvt_d = nc.dram_tensor("wkvt", [1024, 64], F32, kind="ExternalInput")
    wrt_d = nc.dram_tensor("wrt", [1024, 1024], F32, kind="ExternalInput")
    wkt_d = nc.dram_tensor("wkt", [1024, 1024], F32, kind="ExternalInput")
    wvt_d = nc.dram_tensor("wvt", [1024, 1024], F32, kind="ExternalInput")
    mur_d = nc.dram_tensor("mur", [128, 8], F32, kind="ExternalInput")
    muk_d = nc.dram_tensor("muk", [128, 8], F32, kind="ExternalInput")
    ident_d = nc.dram_tensor("ident", [128, 128], F32, kind="ExternalInput")
    y_out = nc.dram_tensor("y", [NSEQ * S, C], F32, kind="ExternalOutput")
    if DEBUG:
        dbg_wtq = nc.dram_tensor("dbg_wtq", [128, CH * 1024], BF16, kind="ExternalOutput")
        dbg_wto = nc.dram_tensor("dbg_wto", [128, 4 * 1024], BF16, kind="ExternalOutput")
        dbg_wtr = nc.dram_tensor("dbg_wtr", [128, CH * 1024], BF16, kind="ExternalOutput")
        dbg_wtkv = nc.dram_tensor("dbg_wtkv", [128, CH * 128], BF16, kind="ExternalOutput")
        dbg_xn = nc.dram_tensor("dbg_xn", [128, CH * XNC], BF16, kind="ExternalOutput")
        dbg_x1 = nc.dram_tensor("dbg_x1", [128, CH * S], F32, kind="ExternalOutput")
        dbg_x2 = nc.dram_tensor("dbg_x2", [128, CH * S], F32, kind="ExternalOutput")

    es = contextlib.ExitStack()
    T_ = es.enter_context
    x_fm = T_(nc.sbuf_tensor("x_fm", [128, CH * S], F32))
    xn = T_(nc.sbuf_tensor("xn", [128, CH * XNC], BF16))
    wt_r = T_(nc.sbuf_tensor("wt_r", [128, CH * 1024], BF16))
    wt_k = T_(nc.sbuf_tensor("wt_k", [128, CH * 1024], BF16))
    wt_v = T_(nc.sbuf_tensor("wt_v", [128, CH * 1024], BF16))
    wt_q = T_(nc.sbuf_tensor("wt_q", [128, CH * 1024], BF16))
    wt_o = T_(nc.sbuf_tensor("wt_o", [128, 4 * 1024], BF16))
    wt_kv = T_(nc.sbuf_tensor("wt_kv", [128, CH * 128], BF16))
    mu_r = T_(nc.sbuf_tensor("mu_r", [128, 8], F32))
    mu_k = T_(nc.sbuf_tensor("mu_k", [128, 8], F32))
    identf = T_(nc.sbuf_tensor("identf", [128, 128], F32))
    identb = T_(nc.sbuf_tensor("identb", [128, 128], BF16))
    ln_ones = T_(nc.sbuf_tensor("ln_ones", [128, 1], BF16))
    ones_row = T_(nc.sbuf_tensor("ones_row", [1, 128], F32))
    eps_col = T_(nc.sbuf_tensor("eps_col", [128, 1], F32))
    ebias_col = T_(nc.sbuf_tensor("ebias_col", [128, 1], F32))
    psA = T_(nc.psum_tensor("psA", [128, 1024], F32))
    psB = T_(nc.psum_tensor("psB", [128, 1024], F32))
    psQ = T_(nc.psum_tensor("psQ", [128, 512], F32))
    psx = [T_(nc.psum_tensor(f"psx{i}", [128, 512], F32)) for i in range(2)]
    ps4 = T_(nc.psum_tensor("ps4", [128, 512], F32))
    ps = [psA[:, 0:512], psA[:, 512:1024], psB[:, 0:512], psB[:, 512:1024]]

    S_ = Sched(nc, es)
    uid = [0]

    def u():
        uid[0] += 1
        return uid[0]

    def t_mm(out, lhsT, rhs, start, stop, waits=(), inc=False):
        return S_.emit("tensor", lambda t: t.matmul(out, lhsT, rhs, start=start, stop=stop),
                       waits, inc=inc)

    def t_tr(out, in_, ident, waits=(), inc=False):
        return S_.emit("tensor", lambda t: t.transpose(out, in_, ident), waits, inc=inc)

    def v_op(fn, waits=(), inc=False):
        return S_.emit("vector", fn, waits, inc=inc)

    def s_op(fn, waits=(), inc=False):
        return S_.emit("scalar", fn, waits, inc=inc)

    def g_op(fn, waits=(), inc=False):
        return S_.emit("gpsimd", fn, waits, inc=inc)

    def g_dma(out, in_, waits=()):
        return S_.emit("gpsimd", lambda g: g.dma_start(out, in_), waits, dma=True)

    def sy_dma(out, in_, waits=()):
        return S_.emit("sync", lambda sy: sy.dma_start(out, in_), waits, dma=True)

    def sc_dma(out, in_, waits=()):
        return S_.emit("scalar", lambda sc: sc.dma_start(out, in_), waits, dma=True)

    with contextlib.ExitStack() as p0:
        P = p0.enter_context
        wq_st = [P(nc.sbuf_tensor(f"wq_st{i}", [128, 2 * 1024], BF16)) for i in range(2)]
        wk_st = [P(nc.sbuf_tensor(f"wk_st{i}", [128, 2 * 128], BF16)) for i in range(2)]
        wo_st = [P(nc.sbuf_tensor(f"wo_st{i}", [128, 2 * 1024], BF16)) for i in range(2)]
        wv_st = [P(nc.sbuf_tensor(f"wv_st{i}", [128, 2 * 64], BF16)) for i in range(2)]

        tok_wr = g_dma(wt_r[:, :].rearrange("p (kc m) -> p kc m", kc=CH),
                       wrt_d.ap().rearrange("(kc p) m -> p kc m", p=128))
        tok_wk = g_dma(wt_k[:, :].rearrange("p (kc m) -> p kc m", kc=CH),
                       wkt_d.ap().rearrange("(kc p) m -> p kc m", p=128))
        tok_wv = g_dma(wt_v[:, :].rearrange("p (kc m) -> p kc m", kc=CH),
                       wvt_d.ap().rearrange("(kc p) m -> p kc m", p=128))
        tok_kv = []
        for half in range(2):
            tok_kv.append(g_dma(
                wt_kv[:, :].rearrange("p (kc l) -> p kc l", kc=CH)[:, :, half * 64:(half + 1) * 64],
                wkvt_d.ap().rearrange("(kc p) l -> p kc l", p=128)))
        tok_idb = g_dma(identb[:, :], ident_d.ap()[:, :])
        tok_idf = sy_dma(identf[:, :], ident_d.ap()[:, :])
        tok_mur = sy_dma(mu_r[:, :], mur_d.ap()[:, :])
        tok_muk = sy_dma(mu_k[:, :], muk_d.ap()[:, :])
        g_op(lambda g: g.memset(ln_ones[:, :], 1.0 / 1024.0), inc=True)
        g_op(lambda g: g.memset(ones_row[:, :], 1.0), inc=True)
        g_op(lambda g: g.memset(eps_col[:, :], 1e-5), inc=True)
        g_op(lambda g: g.memset(ebias_col[:, :], EXP_BIAS), inc=True)

        ev_tok = {}
        prev_mm = {}
        for hp in range(H // 2):
            for h in (2 * hp, 2 * hp + 1):
                b = h % 2
                g_dma(wq_st[b][:, :].rearrange("p (k m) -> p k m", k=2),
                      wq_d.ap()[h * 256:(h + 1) * 256, :].rearrange("(k p) m -> p k m", p=128),
                      waits=[ev_tok.get(("q", h - 2))])
                for k in range(2):
                    for half in range(2):
                        g_dma(
                            wk_st[b][:, k * 128 + half * 64: k * 128 + (half + 1) * 64],
                            wkup_d.ap()[h * 256 + k * 128: h * 256 + (k + 1) * 128, :],
                            waits=[ev_tok.get(("q", h - 2))])
            fl = S_.flush_dma("gpsimd")
            for h in (2 * hp, 2 * hp + 1):
                b = h % 2
                for kc in range(CH):
                    pbank = ps[kc % 2]
                    mmw = [fl, ev_tok.get(("qe", h, kc - 2)) if kc >= 2
                           else ev_tok.get(("qe", h - 1, kc + 6))]
                    mt = None
                    for k in range(2):
                        mt = t_mm(pbank[0:128, 0:128],
                                  wq_st[b][:, k * 1024 + kc * 128: k * 1024 + (kc + 1) * 128],
                                  wk_st[b][:, k * 128:(k + 1) * 128],
                                  start=(k == 0), stop=(k == 1),
                                  waits=mmw if k == 0 else (), inc=(k == 1))
                    ev = s_op(lambda s, pbank=pbank, kc=kc, h=h:
                              s.activation(wt_q[:, kc * 1024 + h * 128: kc * 1024 + (h + 1) * 128],
                                           pbank[0:128, 0:128], AF.Copy, scale=1.0 / 32.0),
                              waits=[mt], inc=True)
                    ev_tok[("qe", h, kc)] = ev
                ev_tok[("q", h)] = ev_tok[("qe", h, CH - 1)]

        for hp in range(H // 2):
            for h in (2 * hp, 2 * hp + 1):
                b = h % 2
                g_dma(wo_st[b][:, :].rearrange("p (k m) -> p k m", k=2),
                      wot_d.ap()[h * 256:(h + 1) * 256, :].rearrange("(k p) m -> p k m", p=128),
                      waits=[ev_tok.get(("o", h - 2))])
                for k in range(2):
                    g_dma(wv_st[b][:, k * 64:(k + 1) * 64],
                          wvup_d.ap()[h * 256 + k * 128: h * 256 + (k + 1) * 128, :],
                          waits=[ev_tok.get(("o", h - 2))])
            fl = S_.flush_dma("gpsimd")
            for h in (2 * hp, 2 * hp + 1):
                b = h % 2
                ev_last = None
                for n in range(2):
                    pbank = ps[2 + n]
                    mt = None
                    mmw = [fl, ev_tok.get(("oe", h - 1, n))]
                    for k in range(2):
                        mt = t_mm(pbank[0:64, 0:512],
                                  wv_st[b][:, k * 64:(k + 1) * 64],
                                  wo_st[b][:, k * 1024 + n * 512: k * 1024 + (n + 1) * 512],
                                  start=(k == 0), stop=(k == 1),
                                  waits=mmw if k == 0 else (), inc=(k == 1))
                    ev = s_op(lambda s, pbank=pbank, h=h, n=n:
                              s.activation(wt_o[(h % 2) * 64:(h % 2) * 64 + 64,
                                                (h // 2) * 1024 + n * 512:(h // 2) * 1024 + (n + 1) * 512],
                                           pbank[0:64, 0:512], AF.Copy),
                              waits=[mt], inc=True)
                    ev_tok[("oe", h, n)] = ev
                    ev_last = ev
                ev_tok[("o", h)] = ev_last

        S_.barrier()

    def phase_in(seq):
        with contextlib.ExitStack() as pp:
            io_tm = [pp.enter_context(nc.sbuf_tensor(f"in_tm{seq}_{i}", [128, 4 * 1024], F32))
                     for i in range(2)]
            cp_tok = {}
            for g4 in range(4):
                buf = io_tm[g4 % 2]
                sy_dma(buf[:, :].rearrange("p (t m) -> p t m", t=4),
                       x_in.ap()[seq * S + g4 * 512: seq * S + (g4 + 1) * 512, :]
                       .rearrange("(t p) m -> p t m", p=128),
                       waits=[cp_tok.get((g4 - 2, 31))])
                ld = S_.flush_dma("sync")
                for i in range(32):
                    ts4, cc = divmod(i, CH)
                    ts = g4 * 4 + ts4
                    pbank = psx[i % 2]
                    tr = t_mm(pbank[0:128, 0:128],
                              buf[:, ts4 * 1024 + cc * 128: ts4 * 1024 + (cc + 1) * 128],
                              identf[:, :], start=True, stop=True,
                              waits=[ld, cp_tok.get((g4, i - 2)),
                                     cp_tok.get((g4 - 1, i + 30)) if i < 2 else None],
                              inc=True)
                    cp_tok[(g4, i)] = v_op(
                        lambda v, pbank=pbank, cc=cc, ts=ts:
                        v.tensor_copy(x_fm[:, cc * S + ts * 128: cc * S + (ts + 1) * 128],
                                      pbank[0:128, 0:128]),
                        waits=[tr], inc=True)
        S_.barrier()

    def phase_ln(seq):
        k = u()
        with contextlib.ExitStack() as pp:
            P = pp.enter_context
            xb = P(nc.sbuf_tensor(f"lnxb{k}", [128, CH * 512], BF16))
            xq = P(nc.sbuf_tensor(f"lnxq{k}", [128, CH * 512], BF16))
            t_sb = P(nc.sbuf_tensor(f"lnt{k}", [128, 512], F32))
            mrow = P(nc.sbuf_tensor(f"lnm{k}", [1, 512], F32))
            qrow = P(nc.sbuf_tensor(f"lnq{k}", [1, 512], F32))
            arow = [P(nc.sbuf_tensor(f"lna{k}_{i}", [1, 512], F32)) for i in range(2)]
            brow = [P(nc.sbuf_tensor(f"lnb{k}_{i}", [1, 512], F32)) for i in range(2)]
            prev_apply = [None]
            stat_tok = {}
            row_tok = {}
            p1_tok = {}
            ap_tok = {}
            bc_pend = [None]

            def emit_bcast_apply():
                if bc_pend[0] is None:
                    return
                tt, c0, r6, r7, r1_ = bc_pend[0]
                bc_pend[0] = None
                tb = tt % 2
                ba = t_mm(ps4[0:128, 0:512], ones_row[0:1, 0:128], arow[tb][0:1, :],
                          start=True, stop=True, waits=[r6, r1_], inc=True)
                bb = t_mm(psx[0][0:128, 0:512], ones_row[0:1, 0:128], brow[tb][0:1, :],
                          start=True, stop=True, waits=[r7, prev_apply[0]], inc=True)
                xn3 = xn[:, :].rearrange("p (c n) -> p c n", c=CH)
                xf3 = x_fm[:, :].rearrange("p (c n) -> p c n", c=CH)
                a_bc = bass.AP(ps4, 0, [[512, 128], [0, CH], [1, 512]])
                b_bc = bass.AP(psx[0], 0, [[512, 128], [0, CH], [1, 512]])
                p1 = v_op(lambda v, c0=c0, xf3=xf3, a_bc=a_bc:
                          v.tensor_mul(xn3[:, :, 1 + c0: 1 + c0 + 512],
                                       xf3[:, :, c0: c0 + 512], a_bc),
                          waits=[ba], inc=True)
                ap_t = v_op(lambda v, c0=c0, b_bc=b_bc:
                            v.tensor_sub(xn3[:, :, 1 + c0: 1 + c0 + 512],
                                         xn3[:, :, 1 + c0: 1 + c0 + 512], b_bc),
                            waits=[bb, p1], inc=True)
                p1_tok[tt] = p1
                ap_tok[tt] = ap_t
                prev_apply[0] = ap_t

            cast_map = {}

            def emit_casts(tt, gates=()):
                cast_t = []
                sq_t = []
                ct = v_op(lambda v, tt=tt:
                          v.tensor_copy(xb[:, :].rearrange("p (c n) -> p c n", c=CH),
                                        x_fm[:, :].rearrange("p (c n) -> p c n", c=CH)
                                        [:, :, tt * 512: tt * 512 + 512]),
                          waits=[stat_tok.get(tt - 1)] + list(gates), inc=True)
                sq = g_op(lambda g:
                          g.tensor_mul(xq[:, :], xb[:, :], xb[:, :]),
                          waits=[ct], inc=True)
                for cc in range(CH):
                    cast_t.append(ct)
                    sq_t.append(sq)
                cast_map[tt] = (cast_t, sq_t)

            emit_casts(0)
            for tt in range(NTT):
                c0 = tt * 512
                tb = tt % 2
                cast_t, sq_t = cast_map[tt]
                mt = None
                for cc in range(CH):
                    mt = t_mm(psx[1][0:1, 0:512], ln_ones[:, 0:1], xb[:, cc * 512:(cc + 1) * 512],
                              start=(cc == 0), stop=(cc == CH - 1),
                              waits=[cast_t[cc], row_tok.get(tt - 1)], inc=(cc == CH - 1))
                qt = None
                for cc in range(CH):
                    qt = t_mm(psx[1][32:33, 0:512], ln_ones[:, 0:1], xq[:, cc * 512:(cc + 1) * 512],
                              start=(cc == 0), stop=(cc == CH - 1),
                              waits=[sq_t[cc]], inc=(cc == CH - 1))
                stat_tok[tt] = qt
                if tt + 1 < NTT:
                    emit_casts(tt + 1)
                emit_bcast_apply()
                r1 = v_op(lambda v: v.tensor_copy(mrow[0:1, :], psx[1][0:1, 0:512]), waits=[mt, qt], inc=True)
                r2 = v_op(lambda v: v.tensor_copy(qrow[0:1, :], psx[1][32:33, 0:512]), waits=[qt], inc=True)
                row_tok[tt] = r2
                r3 = v_op(lambda v, tb=tb: v.tensor_mul(brow[tb][0:1, :], mrow[0:1, :], mrow[0:1, :]),
                          waits=[r1], inc=True)
                r4 = v_op(lambda v, tb=tb: v.tensor_sub(qrow[0:1, :], qrow[0:1, :], brow[tb][0:1, :]),
                          waits=[r2, r3], inc=True)
                r5 = s_op(lambda s, tb=tb: s.activation(brow[tb][0:1, :], qrow[0:1, :], AF.Ln,
                                                        bias=eps_col[0:1, 0:1]),
                          waits=[r4], inc=True)
                r6 = s_op(lambda s, tb=tb: s.activation(arow[tb][0:1, :], brow[tb][0:1, :], AF.Exp,
                                                        scale=-0.5),
                          waits=[r5], inc=True)
                r7 = v_op(lambda v, tb=tb: v.tensor_mul(brow[tb][0:1, :], mrow[0:1, :], arow[tb][0:1, :]),
                          waits=[r6], inc=True)
                bc_pend[0] = (tt, c0, r6, r7, r1)
            emit_bcast_apply()
        S_.barrier()

    def phase_ffn(seq):
        k = u()
        with contextlib.ExitStack() as pp:
            P = pp.enter_context
            d_t = P(nc.sbuf_tensor(f"ffnd{k}", [128, 512], BF16))
            xr = P(nc.sbuf_tensor(f"ffnxr{k}", [128, CH * 512], BF16))
            xk = P(nc.sbuf_tensor(f"ffnxk{k}", [128, CH * 512], BF16))
            r_sig = P(nc.sbuf_tensor(f"ffnrs{k}", [128, CH * 512], BF16))
            krelu = P(nc.sbuf_tensor(f"ffnkr{k}", [128, 512], BF16))
            k_act = P(nc.sbuf_tensor(f"ffnka{k}", [128, CH * 512], BF16))
            delta = P(nc.sbuf_tensor(f"ffndel{k}", [128, 512], F32))

            prev_de = {}
            prev_ka = {}
            prev_mm = {"r": None, "k": None, "v": None, "xk": None}
            for tt in range(NTT):
                c0 = tt * 512
                mix_r = []
                mix_k = []
                xn3 = xn[:, :].rearrange("p (c n) -> p c n", c=CH)
                dt = v_op(lambda v, c0=c0:
                          v.tensor_sub(k_act[:, :].rearrange("p (c n) -> p c n", c=CH),
                                       xn3[:, :, 1 + c0: 1 + c0 + 512],
                                       xn3[:, :, c0: c0 + 512]),
                          waits=[prev_mm["r"], prev_mm["k"], prev_mm["v"]], inc=True)
                for cc in range(CH):
                    prv = (cc * XNC + c0, cc * XNC + c0 + 512)
                    rt = v_op(lambda v, cc=cc, prv=prv:
                              v.scalar_tensor_tensor(xr[:, cc * 512:(cc + 1) * 512],
                                                     k_act[:, cc * 512:(cc + 1) * 512],
                                                     mu_r[:, cc:cc + 1], xn[:, prv[0]:prv[1]],
                                                     ALU.mult, ALU.add),
                              waits=[dt], inc=True)
                    kt_ = v_op(lambda v, cc=cc, prv=prv:
                               v.scalar_tensor_tensor(xk[:, cc * 512:(cc + 1) * 512],
                                                      k_act[:, cc * 512:(cc + 1) * 512],
                                                      mu_k[:, cc:cc + 1], xn[:, prv[0]:prv[1]],
                                                      ALU.mult, ALU.add),
                               waits=[dt], inc=True)
                    mix_r.append(rt)
                    mix_k.append(kt_)
                sig_t = []
                for oc in range(CH):
                    pbank = ps[oc % 2]
                    mt = None
                    for cc in range(CH):
                        mt = t_mm(pbank[0:128, 0:512],
                                  wt_r[:, cc * 1024 + oc * 128: cc * 1024 + (oc + 1) * 128],
                                  xr[:, cc * 512:(cc + 1) * 512],
                                  start=(cc == 0), stop=(cc == CH - 1),
                                  waits=([mix_r[-1],
                                          sig_t[oc - 2] if oc >= 2 else prev_de.get(oc + 6)]
                                         if cc == 0 else ()),
                                  inc=(cc == CH - 1))
                    st = s_op(lambda s, pbank=pbank, oc=oc:
                              s.activation(r_sig[:, oc * 512:(oc + 1) * 512], pbank[0:128, 0:512],
                                           AF.Sigmoid),
                              waits=[mt, prev_de.get(oc)], inc=True)
                    sig_t.append(st)
                ka_t = []
                for oc in range(CH):
                    pbank = ps[oc % 2]
                    mt = None
                    for cc in range(CH):
                        mt = t_mm(pbank[0:128, 0:512],
                                  wt_k[:, cc * 1024 + oc * 128: cc * 1024 + (oc + 1) * 128],
                                  xk[:, cc * 512:(cc + 1) * 512],
                                  start=(cc == 0), stop=(cc == CH - 1),
                                  waits=([mix_k[-1],
                                          ka_t[oc - 2] if oc >= 2 else sig_t[oc + 6]]
                                         if cc == 0 else ()),
                                  inc=(cc == CH - 1))
                    rl = s_op(lambda s, pbank=pbank:
                              s.activation(krelu[:, :], pbank[0:128, 0:512], AF.Relu),
                              waits=[mt, ka_t[oc - 1] if oc >= 1 else prev_ka.get(7)], inc=True)
                    ka = v_op(lambda v, pbank=pbank, oc=oc:
                              v.tensor_mul(k_act[:, oc * 512:(oc + 1) * 512], pbank[0:128, 0:512],
                                           krelu[:, :]),
                              waits=[rl], inc=True)
                    ka_t.append(ka)
                add_t = []
                for oc in range(CH):
                    pbank = ps[oc % 2]
                    mt = None
                    for cc in range(CH):
                        mt = t_mm(pbank[0:128, 0:512],
                                  wt_v[:, cc * 1024 + oc * 128: cc * 1024 + (oc + 1) * 128],
                                  k_act[:, cc * 512:(cc + 1) * 512],
                                  start=(cc == 0), stop=(cc == CH - 1),
                                  waits=([ka_t[CH - 1], sig_t[CH - 1],
                                          add_t[oc - 2] if oc >= 2 else None]
                                         if cc == 0 else ()),
                                  inc=(cc == CH - 1))
                    de = v_op(lambda v, pbank=pbank, oc=oc:
                              v.scalar_tensor_tensor(delta[:, :], pbank[0:128, 0:512], 0.5,
                                                     r_sig[:, oc * 512:(oc + 1) * 512],
                                                     ALU.mult, ALU.mult),
                              waits=[mt], inc=True)
                    at = v_op(lambda v, oc=oc, c0=c0:
                              v.tensor_add(x_fm[:, oc * S + c0: oc * S + c0 + 512],
                                           x_fm[:, oc * S + c0: oc * S + c0 + 512], delta[:, :]),
                              waits=[de], inc=True)
                    add_t.append(at)
                    prev_de[oc] = de
                for oc in range(CH):
                    prev_ka[oc] = ka_t[oc]
                prev_mm["r"] = sig_t[CH - 1]
                prev_mm["k"] = ka_t[CH - 1]
                prev_mm["v"] = mt
        S_.barrier()

    def phase_attn(seq):
        with contextlib.ExitStack() as pp:
            P = pp.enter_context
            latent = P(nc.sbuf_tensor(f"lat{seq}", [128, S], BF16))
            lat_tm = P(nc.sbuf_tensor(f"lattm{seq}", [128, NKT * 128], BF16))
            qd = [P(nc.sbuf_tensor(f"qd{seq}_{i}", [128, 512], BF16)) for i in range(2)]
            expT = [P(nc.sbuf_tensor(f"expT{seq}_{i}", [128, 1024], BF16)) for i in range(2)]
            u_fm = P(nc.sbuf_tensor(f"ufm{seq}", [128, 2 * 4 * 512], BF16))
            row_a = P(nc.sbuf_tensor(f"rowa{seq}", [1, 512], F32))
            row_b = P(nc.sbuf_tensor(f"rowb{seq}", [1, 512], F32))
            row_c = P(nc.sbuf_tensor(f"rowc{seq}", [1, 512], F32))
            rb = P(nc.sbuf_tensor(f"rb{seq}", [64, 512], F32))

            lat_ev = []
            for qg in range(NQG):
                pbank = ps[qg % 2]
                mt = None
                for cc in range(CH):
                    mt = t_mm(pbank[0:128, 0:512],
                              wt_kv[:, cc * 128:(cc + 1) * 128],
                              xn[:, cc * XNC + 1 + qg * 512: cc * XNC + 1 + qg * 512 + 512],
                              start=(cc == 0), stop=(cc == CH - 1),
                              waits=([lat_ev[qg - 2] if qg >= 2 else None] if cc == 0 else ()),
                              inc=(cc == CH - 1))
                ev = s_op(lambda s, pbank=pbank, qg=qg:
                          s.activation(latent[:, qg * 512:(qg + 1) * 512], pbank[0:128, 0:512],
                                       AF.Copy),
                          waits=[mt], inc=True)
                lat_ev.append(ev)
            zpad_t = g_op(lambda g: g.memset(lat_tm[:, :], 0.0), inc=True)
            tm_cp = {}
            for kt in range(NKT):
                S_.emit("sync",
                        lambda sy, kt=kt: sy.dma_start_transpose(
                            lat_tm[:, kt * 128: kt * 128 + 64],
                            latent[0:64, kt * 128:(kt + 1) * 128]),
                        waits=[lat_ev[kt // 4], zpad_t], dma=True)
                tm_cp[kt] = S_.flush_dma("sync") if kt == NKT - 1 else None
            ones_t = g_op(lambda g: g.memset(
                lat_tm[:, :].rearrange("p (kt c) -> p kt c", kt=NKT)[:, :, 64:65], 1.0),
                waits=[zpad_t], inc=True)

            u_ev = {}
            add_t = {}
            qd_ev = {}
            prev_c5 = [None]
            norm_pend = [None]

            def emit_norm():
                if norm_pend[0] is None:
                    return
                nqg, nh, nb, num = norm_pend[0]
                norm_pend[0] = None
                c1 = v_op(lambda v, nb=nb: v.tensor_copy(row_a[0:1, :], psx[nb][64:65, 0:512]),
                          waits=[num], inc=True)
                c3 = v_op(lambda v: v.reciprocal(row_b[0:1, :], row_a[0:1, :]),
                          waits=[c1], inc=True)
                c4 = t_mm(ps4[0:64, 0:512], ones_row[0:1, 0:64], row_b[0:1, :],
                          start=True, stop=True, waits=[c3, prev_c5[0]], inc=True)
                c5 = v_op(lambda v: v.tensor_copy(rb[0:64, :], ps4[0:64, 0:512]),
                          waits=[c4], inc=True)
                prev_c5[0] = c5
                ub = (nqg % 2) * 2048
                ue = v_op(lambda v, nb=nb, nh=nh, ub=ub:
                          v.tensor_mul(u_fm[(nh % 2) * 64:(nh % 2) * 64 + 64,
                                            ub + (nh // 2) * 512: ub + (nh // 2) * 512 + 512],
                                       psx[nb][0:64, 0:512], rb[0:64, :]),
                          waits=[c5], inc=True)
                u_ev[(nqg, nh)] = ue

            qt_mt = {}
            bank_free = [None]

            def emit_qt(qg, h):
                q0 = qg * 512
                mt = None
                for cc in range(CH):
                    mt = t_mm(psQ[0:128, 0:512],
                              wt_q[:, cc * 1024 + h * 128: cc * 1024 + (h + 1) * 128],
                              xn[:, cc * XNC + 1 + q0: cc * XNC + 1 + q0 + 512],
                              start=(cc == 0), stop=(cc == CH - 1),
                              waits=([bank_free[0]] if cc == 0 else ()),
                              inc=(cc == CH - 1))
                qt_mt[(qg, h)] = mt

            def emit_qe(qg, h):
                qe = v_op(lambda v, h=h: v.tensor_copy(qd[h % 2][:, :], psQ[0:128, 0:512]),
                          waits=[qt_mt[(qg, h)]], inc=True)
                qd_ev[(qg, h)] = qe
                bank_free[0] = qe

            def emit_out_grp(oqg, oc):
                oq0 = oqg * 512
                ub = (oqg % 2) * 2048
                mt = None
                for uc in range(4):
                    mt = t_mm(psQ[0:128, 0:512],
                              wt_o[:, uc * 1024 + oc * 128: uc * 1024 + (oc + 1) * 128],
                              u_fm[:, ub + uc * 512: ub + (uc + 1) * 512],
                              start=(uc == 0), stop=(uc == 3),
                              waits=([u_ev[(oqg, hh)] for hh in range(H)] +
                                     [bank_free[0]] if uc == 0 else ()),
                              inc=(uc == 3))
                at = v_op(lambda v, oc=oc, oq0=oq0:
                          v.tensor_add(x_fm[:, oc * S + oq0: oc * S + oq0 + 512],
                                       x_fm[:, oc * S + oq0: oc * S + oq0 + 512],
                                       psQ[0:128, 0:512]),
                          waits=[mt], inc=True)
                add_t[(oqg, oc)] = at
                bank_free[0] = at

            for qg in range(NQG):
                q0 = qg * 512
                for h in range(H):
                    b = h % 2
                    if (qg, h) not in qt_mt:
                        emit_qt(qg, h)
                    if (qg, h) not in qd_ev:
                        emit_qe(qg, h)
                    qe = qd_ev[(qg, h)]
                    exp_t = {}
                    NP = NKT // 2

                    def emit_u(j, qg=qg, h=h, b=b, exp_t=exp_t):
                        um = t_mm(psx[b][0:128, 0:512], lat_tm[:, j * 128: j * 128 + 128],
                                  expT[(j // 2) % 2][:, (j % 2) * 512:(j % 2) * 512 + 512],
                                  start=(j == 0), stop=(j == NKT - 1),
                                  waits=[exp_t[j // 2]] + ([u_ev.get((qg, h - 2)),
                                                            u_ev.get((qg - 1, h + 6)) if h < 2 else None]
                                                           if j == 0 else []),
                                  inc=True)
                        exp_t[("u", j)] = um
                        return um

                    for p in range(NP):
                        pt = psA if p % 2 == 0 else psB
                        st = None
                        for half in range(2):
                            kt = 2 * p + half
                            st = t_mm(pt[0:128, half * 512: half * 512 + 512],
                                      latent[:, kt * 128:(kt + 1) * 128], qd[b][:, :],
                                      start=True, stop=True,
                                      waits=[qe,
                                             exp_t.get(p - 2) if half == 0 else None,
                                             (tm_cp[NKT - 1] if (qg == 0 and h == 0 and p == 0) else None),
                                             (ones_t if (qg == 0 and h == 0 and p == 0) else None)],
                                      inc=True)
                        et = s_op(lambda s, pt=pt, p=p:
                                  s.activation(expT[p % 2][:, 0:1024], pt[0:128, 0:1024],
                                               AF.Exp, bias=ebias_col[:, 0:1]),
                                  waits=[st, exp_t.get(("u", 2 * (p - 2) + 1))], inc=True)
                        exp_t[p] = et
                        if p >= 1:
                            emit_u(2 * (p - 1))
                            emit_u(2 * (p - 1) + 1)
                        nh = h + 1
                        nqg2 = qg if nh < H else qg + 1
                        nh = nh if nh < H else 0
                        if p == 1 and nqg2 < NQG:
                            emit_qt(nqg2, nh)
                        if p == 2:
                            emit_norm()
                        if p == 3 and nqg2 < NQG:
                            emit_qe(nqg2, nh)
                        if p == 4 and qg >= 1:
                            emit_out_grp(qg - 1, h)
                    emit_u(NKT - 2)
                    um = emit_u(NKT - 1)
                    norm_pend[0] = (qg, h, b, um)
            emit_norm()
            for oc in range(CH):
                emit_out_grp(NQG - 1, oc)
        S_.barrier()

    def phase_out(seq):
        with contextlib.ExitStack() as pp:
            io_tm = [pp.enter_context(nc.sbuf_tensor(f"out_tm{seq}_{i}", [128, 4 * 1024], F32))
                     for i in range(2)]
            st_tok = {}
            cp_tok = {}
            for g4 in range(4):
                buf = io_tm[g4 % 2]
                cps = []
                for i in range(32):
                    ts4, cc = divmod(i, CH)
                    ts = g4 * 4 + ts4
                    pbank = ps[i % 2]
                    tr = t_mm(pbank[0:128, 0:128],
                              x_fm[:, cc * S + ts * 128: cc * S + (ts + 1) * 128], identf[:, :],
                              start=True, stop=True,
                              waits=[cp_tok.get((g4, i - 2)),
                                     cp_tok.get((g4 - 1, i + 30)) if i < 2 else None],
                              inc=True)
                    cp = v_op(lambda v, pbank=pbank, cc=cc, ts4=ts4, buf=buf:
                              v.tensor_copy(buf[:, ts4 * 1024 + cc * 128: ts4 * 1024 + (cc + 1) * 128],
                                            pbank[0:128, 0:128]),
                              waits=[tr, st_tok.get(g4 - 2)], inc=True)
                    cp_tok[(g4, i)] = cp
                    cps.append(cp)
                sy_dma(
                    y_out.ap()[seq * S + g4 * 512: seq * S + (g4 + 1) * 512, :]
                    .rearrange("(t p) m -> p t m", p=128),
                    buf[:, :].rearrange("p (t m) -> p t m", t=4),
                    waits=cps)
                st_tok[g4] = S_.flush_dma("sync")
        S_.barrier()

    g_op(lambda g: g.memset(
        xn[:, :].rearrange("p (c t) -> p c t", c=CH)[:, :, 0:1], 0.0), inc=True)

    def dbg_dump(t, sb):
        S_.emit("sync", lambda sy, t=t, sb=sb: sy.dma_start(t.ap()[:, :], sb[:, :]), dma=True)
        S_.flush_dma("sync")
        S_.barrier()

    for seq in range(NSEQ):
        phase_in(seq)
        phase_ln(seq)
        if DEBUG and seq == 0:
            dbg_dump(dbg_xn, xn)
        phase_ffn(seq)
        if DEBUG and seq == 0:
            dbg_dump(dbg_x1, x_fm)
        phase_ln(seq)
        phase_attn(seq)
        if DEBUG and seq == 0:
            dbg_dump(dbg_x2, x_fm)
        phase_ln(seq)
        phase_ffn(seq)
        phase_out(seq)
    if DEBUG:
        dbg_dump(dbg_wtq, wt_q)
        dbg_dump(dbg_wto, wt_o)
        dbg_dump(dbg_wtr, wt_r)
        dbg_dump(dbg_wtkv, wt_kv)

    S_.flush_dma("sync")
    S_.flush_dma("gpsimd")

    with nc.Block() as blk:
        @blk.tensor
        def _(t):
            for f in S_.ops["tensor"]:
                f(t)

        @blk.vector
        def _(v):
            for f in S_.ops["vector"]:
                f(v)

        @blk.scalar
        def _(s):
            for f in S_.ops["scalar"]:
                f(s)

        @blk.gpsimd
        def _(g):
            for f in S_.ops["gpsimd"]:
                f(g)

        @blk.sync
        def _(sy):
            for f in S_.ops["sync"]:
                f(sy)

    es.close()
    return nc


def kernel(L0_attn_x, L0_attn_Wq, L0_attn_Wkv_down, L0_attn_Wk_up, L0_attn_Wv_up,
           L0_attn_Wo, L0_ffn_W_r, L0_ffn_W_k, L0_ffn_W_v, L0_ffn_mu_r, L0_ffn_mu_k):
    x = np.ascontiguousarray(np.asarray(L0_attn_x, dtype=np.float32))
    shared = {
        "wq": np.ascontiguousarray(np.asarray(L0_attn_Wq, np.float32)),
        "wkup": np.ascontiguousarray(np.asarray(L0_attn_Wk_up, np.float32)),
        "wvup": np.ascontiguousarray(np.asarray(L0_attn_Wv_up, np.float32)),
        "wot": np.ascontiguousarray(np.asarray(L0_attn_Wo, np.float32).T),
        "wkvt": np.ascontiguousarray(np.asarray(L0_attn_Wkv_down, np.float32).T),
        "wrt": np.ascontiguousarray(np.asarray(L0_ffn_W_r, np.float32).T),
        "wkt": np.ascontiguousarray(np.asarray(L0_ffn_W_k, np.float32).T),
        "wvt": np.ascontiguousarray(np.asarray(L0_ffn_W_v, np.float32).T),
        "mur": np.ascontiguousarray(np.asarray(L0_ffn_mu_r, np.float32).reshape(8, 128).T),
        "muk": np.ascontiguousarray(np.asarray(L0_ffn_mu_k, np.float32).reshape(8, 128).T),
        "ident": np.eye(128, dtype=np.float32),
    }
    in_maps = []
    for c in range(NCORES):
        shard = np.ascontiguousarray(x[c * NSEQ:(c + 1) * NSEQ].reshape(NSEQ * S, C))
        in_maps.append({"x": shard, **shared})
    nc = build_nc()
    res = run_bass_kernel_spmd(nc, in_maps, core_ids=list(range(NCORES)))
    out = np.empty((16, S, C), dtype=np.float32)
    for c in range(NCORES):
        out[c * NSEQ:(c + 1) * NSEQ] = res.results[c]["y"].reshape(NSEQ, S, C)
    return out


# version 25
# speedup vs baseline: 1.1361x; 1.1361x over previous
import contextlib
import numpy as np

import concourse.bass as bass
import concourse.mybir as mybir
from concourse.bass_utils import run_bass_kernel_spmd

F32 = mybir.dt.float32
BF16 = mybir.dt.bfloat16
ALU = mybir.AluOpType
AF = mybir.ActivationFunctionType
AX = mybir.AxisListType

NCORES = 8
NSEQ = 2
S = 2048
C = 1024
CH = 8
TT = 512
NTT = S // TT
H = 8
RK = 64
NKT = S // 128
QG = 512
NQG = S // QG
XNC = S + 1
EXP_BIAS = -20.0

ENGS = ("tensor", "vector", "scalar", "gpsimd", "sync")
DEBUG = False


class Sched:

    def __init__(self, nc, es):
        self.nc = nc
        self.ops = {e: [] for e in ENGS}
        self.msem = {e: es.enter_context(nc.semaphore(f"ms_{e}")) for e in ENGS}
        self.dsem = {e: es.enter_context(nc.semaphore(f"ds_{e}")) for e in ("gpsimd", "sync", "scalar")}
        self.mcnt = {e: 0 for e in ENGS}
        self.dcnt = {e: 0 for e in self.dsem}
        self.waited = {e: {} for e in ENGS}
        self.pending = {e: [] for e in ENGS}

    def _resolve(self, eng, waits):
        out = []
        for tok in waits:
            if tok is None:
                continue
            kind, weng, val = tok
            if kind == "m" and weng == eng:
                continue
            key = (kind, weng)
            if self.waited[eng].get(key, 0) >= val:
                continue
            self.waited[eng][key] = val
            sem = self.msem[weng] if kind == "m" else self.dsem[weng]
            out.append((sem, val))
        return out

    def emit(self, eng, fn, waits=(), inc=False, dma=False):
        allw = list(self.pending[eng]) + list(waits)
        self.pending[eng] = []
        wl = self._resolve(eng, allw)
        tok = None
        sem = None
        amt = 0
        if dma:
            self.dcnt[eng] += 16
            tok = ("d", eng, self.dcnt[eng])
            sem, amt = self.dsem[eng], 16
        elif inc:
            self.mcnt[eng] += 1
            tok = ("m", eng, self.mcnt[eng])
            sem, amt = self.msem[eng], 1

        def clos(e, fn=fn, wl=wl, sem=sem, amt=amt):
            for s_, v_ in wl:
                e.wait_ge(s_, v_)
            ins = fn(e)
            if sem is not None:
                ins.then_inc(sem, amt)

        self.ops[eng].append(clos)
        return tok

    def flush_dma(self, eng):
        if self.dcnt.get(eng, 0) == 0:
            return None
        tok = ("d", eng, self.dcnt[eng])
        return self.emit(eng, lambda e: e.nop(), waits=[tok], inc=True)

    def barrier(self):
        toks = [("m", e, self.mcnt[e]) for e in ENGS if self.mcnt[e] > 0]
        toks += [("d", e, self.dcnt[e]) for e in self.dcnt if self.dcnt[e] > 0]
        for e in ENGS:
            self.pending[e].extend(toks)


def build_nc():
    nc = bass.Bass("TRN2", target_bir_lowering=False)

    x_in = nc.dram_tensor("x", [NSEQ * S, C], F32, kind="ExternalInput")
    wq_d = nc.dram_tensor("wq", [2048, 1024], F32, kind="ExternalInput")
    wkup_d = nc.dram_tensor("wkup", [2048, 64], F32, kind="ExternalInput")
    wvup_d = nc.dram_tensor("wvup", [2048, 64], F32, kind="ExternalInput")
    wot_d = nc.dram_tensor("wot", [2048, 1024], F32, kind="ExternalInput")
    wkvt_d = nc.dram_tensor("wkvt", [1024, 64], F32, kind="ExternalInput")
    wrt_d = nc.dram_tensor("wrt", [1024, 1024], F32, kind="ExternalInput")
    wkt_d = nc.dram_tensor("wkt", [1024, 1024], F32, kind="ExternalInput")
    wvt_d = nc.dram_tensor("wvt", [1024, 1024], F32, kind="ExternalInput")
    mur_d = nc.dram_tensor("mur", [128, 8], F32, kind="ExternalInput")
    muk_d = nc.dram_tensor("muk", [128, 8], F32, kind="ExternalInput")
    ident_d = nc.dram_tensor("ident", [128, 128], F32, kind="ExternalInput")
    y_out = nc.dram_tensor("y", [NSEQ * S, C], F32, kind="ExternalOutput")
    if DEBUG:
        dbg_wtq = nc.dram_tensor("dbg_wtq", [128, CH * 1024], BF16, kind="ExternalOutput")
        dbg_wto = nc.dram_tensor("dbg_wto", [128, 4 * 1024], BF16, kind="ExternalOutput")
        dbg_wtr = nc.dram_tensor("dbg_wtr", [128, CH * 1024], BF16, kind="ExternalOutput")
        dbg_wtkv = nc.dram_tensor("dbg_wtkv", [128, CH * 128], BF16, kind="ExternalOutput")
        dbg_xn = nc.dram_tensor("dbg_xn", [128, CH * XNC], BF16, kind="ExternalOutput")
        dbg_x1 = nc.dram_tensor("dbg_x1", [128, CH * S], F32, kind="ExternalOutput")
        dbg_x2 = nc.dram_tensor("dbg_x2", [128, CH * S], F32, kind="ExternalOutput")

    es = contextlib.ExitStack()
    T_ = es.enter_context
    x_fm = T_(nc.sbuf_tensor("x_fm", [128, CH * S], F32))
    xn = T_(nc.sbuf_tensor("xn", [128, CH * XNC], BF16))
    wt_r = T_(nc.sbuf_tensor("wt_r", [128, CH * 1024], BF16))
    wt_k = T_(nc.sbuf_tensor("wt_k", [128, CH * 1024], BF16))
    wt_v = T_(nc.sbuf_tensor("wt_v", [128, CH * 1024], BF16))
    wt_q = T_(nc.sbuf_tensor("wt_q", [128, CH * 1024], BF16))
    wt_o = T_(nc.sbuf_tensor("wt_o", [128, 4 * 1024], BF16))
    wt_kv = T_(nc.sbuf_tensor("wt_kv", [128, CH * 128], BF16))
    mu_r = T_(nc.sbuf_tensor("mu_r", [128, 8], F32))
    mu_k = T_(nc.sbuf_tensor("mu_k", [128, 8], F32))
    identf = T_(nc.sbuf_tensor("identf", [128, 128], F32))
    identb = T_(nc.sbuf_tensor("identb", [128, 128], BF16))
    ln_ones = T_(nc.sbuf_tensor("ln_ones", [128, 1], BF16))
    ones_row = T_(nc.sbuf_tensor("ones_row", [1, 128], F32))
    eps_col = T_(nc.sbuf_tensor("eps_col", [128, 1], F32))
    ebias_col = T_(nc.sbuf_tensor("ebias_col", [128, 1], F32))
    psA = T_(nc.psum_tensor("psA", [128, 1024], F32))
    psB = T_(nc.psum_tensor("psB", [128, 1024], F32))
    psQ = T_(nc.psum_tensor("psQ", [128, 512], F32))
    psX = T_(nc.psum_tensor("psX", [128, 1024], F32))
    psx = [psX[:, 0:512], psX[:, 512:1024]]
    ps4 = T_(nc.psum_tensor("ps4", [128, 512], F32))
    ps = [psA[:, 0:512], psA[:, 512:1024], psB[:, 0:512], psB[:, 512:1024]]

    S_ = Sched(nc, es)
    uid = [0]

    def u():
        uid[0] += 1
        return uid[0]

    def t_mm(out, lhsT, rhs, start, stop, waits=(), inc=False):
        return S_.emit("tensor", lambda t: t.matmul(out, lhsT, rhs, start=start, stop=stop),
                       waits, inc=inc)

    def t_tr(out, in_, ident, waits=(), inc=False):
        return S_.emit("tensor", lambda t: t.transpose(out, in_, ident), waits, inc=inc)

    def v_op(fn, waits=(), inc=False):
        return S_.emit("vector", fn, waits, inc=inc)

    def s_op(fn, waits=(), inc=False):
        return S_.emit("scalar", fn, waits, inc=inc)

    def g_op(fn, waits=(), inc=False):
        return S_.emit("gpsimd", fn, waits, inc=inc)

    def g_dma(out, in_, waits=()):
        return S_.emit("gpsimd", lambda g: g.dma_start(out, in_), waits, dma=True)

    def sy_dma(out, in_, waits=()):
        return S_.emit("sync", lambda sy: sy.dma_start(out, in_), waits, dma=True)

    def sc_dma(out, in_, waits=()):
        return S_.emit("scalar", lambda sc: sc.dma_start(out, in_), waits, dma=True)

    with contextlib.ExitStack() as p0:
        P = p0.enter_context
        wq_st = [P(nc.sbuf_tensor(f"wq_st{i}", [128, 2 * 1024], BF16)) for i in range(2)]
        wk_st = [P(nc.sbuf_tensor(f"wk_st{i}", [128, 2 * 128], BF16)) for i in range(2)]
        wo_st = [P(nc.sbuf_tensor(f"wo_st{i}", [128, 2 * 1024], BF16)) for i in range(2)]
        wv_st = [P(nc.sbuf_tensor(f"wv_st{i}", [128, 2 * 64], BF16)) for i in range(2)]

        tok_wr = g_dma(wt_r[:, :].rearrange("p (kc m) -> p kc m", kc=CH),
                       wrt_d.ap().rearrange("(kc p) m -> p kc m", p=128))
        tok_wk = g_dma(wt_k[:, :].rearrange("p (kc m) -> p kc m", kc=CH),
                       wkt_d.ap().rearrange("(kc p) m -> p kc m", p=128))
        tok_wv = g_dma(wt_v[:, :].rearrange("p (kc m) -> p kc m", kc=CH),
                       wvt_d.ap().rearrange("(kc p) m -> p kc m", p=128))
        tok_kv = []
        for half in range(2):
            tok_kv.append(g_dma(
                wt_kv[:, :].rearrange("p (kc l) -> p kc l", kc=CH)[:, :, half * 64:(half + 1) * 64],
                wkvt_d.ap().rearrange("(kc p) l -> p kc l", p=128)))
        tok_idb = g_dma(identb[:, :], ident_d.ap()[:, :])
        tok_idf = sy_dma(identf[:, :], ident_d.ap()[:, :])
        tok_mur = sy_dma(mu_r[:, :], mur_d.ap()[:, :])
        tok_muk = sy_dma(mu_k[:, :], muk_d.ap()[:, :])
        g_op(lambda g: g.memset(ln_ones[:, :], 1.0 / 1024.0), inc=True)
        g_op(lambda g: g.memset(ones_row[:, :], 1.0), inc=True)
        g_op(lambda g: g.memset(eps_col[:, :], 1e-5), inc=True)
        g_op(lambda g: g.memset(ebias_col[:, :], EXP_BIAS), inc=True)

        ev_tok = {}
        prev_mm = {}
        for hp in range(H // 2):
            for h in (2 * hp, 2 * hp + 1):
                b = h % 2
                g_dma(wq_st[b][:, :].rearrange("p (k m) -> p k m", k=2),
                      wq_d.ap()[h * 256:(h + 1) * 256, :].rearrange("(k p) m -> p k m", p=128),
                      waits=[ev_tok.get(("q", h - 2))])
                for k in range(2):
                    for half in range(2):
                        g_dma(
                            wk_st[b][:, k * 128 + half * 64: k * 128 + (half + 1) * 64],
                            wkup_d.ap()[h * 256 + k * 128: h * 256 + (k + 1) * 128, :],
                            waits=[ev_tok.get(("q", h - 2))])
            fl = S_.flush_dma("gpsimd")
            for h in (2 * hp, 2 * hp + 1):
                b = h % 2
                for kc in range(CH):
                    pbank = ps[kc % 2]
                    mmw = [fl, ev_tok.get(("qe", h, kc - 2)) if kc >= 2
                           else ev_tok.get(("qe", h - 1, kc + 6))]
                    mt = None
                    for k in range(2):
                        mt = t_mm(pbank[0:128, 0:128],
                                  wq_st[b][:, k * 1024 + kc * 128: k * 1024 + (kc + 1) * 128],
                                  wk_st[b][:, k * 128:(k + 1) * 128],
                                  start=(k == 0), stop=(k == 1),
                                  waits=mmw if k == 0 else (), inc=(k == 1))
                    ev = s_op(lambda s, pbank=pbank, kc=kc, h=h:
                              s.activation(wt_q[:, kc * 1024 + h * 128: kc * 1024 + (h + 1) * 128],
                                           pbank[0:128, 0:128], AF.Copy, scale=1.0 / 32.0),
                              waits=[mt], inc=True)
                    ev_tok[("qe", h, kc)] = ev
                ev_tok[("q", h)] = ev_tok[("qe", h, CH - 1)]

        for hp in range(H // 2):
            for h in (2 * hp, 2 * hp + 1):
                b = h % 2
                g_dma(wo_st[b][:, :].rearrange("p (k m) -> p k m", k=2),
                      wot_d.ap()[h * 256:(h + 1) * 256, :].rearrange("(k p) m -> p k m", p=128),
                      waits=[ev_tok.get(("o", h - 2))])
                for k in range(2):
                    g_dma(wv_st[b][:, k * 64:(k + 1) * 64],
                          wvup_d.ap()[h * 256 + k * 128: h * 256 + (k + 1) * 128, :],
                          waits=[ev_tok.get(("o", h - 2))])
            fl = S_.flush_dma("gpsimd")
            for h in (2 * hp, 2 * hp + 1):
                b = h % 2
                ev_last = None
                for n in range(2):
                    pbank = ps[2 + n]
                    mt = None
                    mmw = [fl, ev_tok.get(("oe", h - 1, n))]
                    for k in range(2):
                        mt = t_mm(pbank[0:64, 0:512],
                                  wv_st[b][:, k * 64:(k + 1) * 64],
                                  wo_st[b][:, k * 1024 + n * 512: k * 1024 + (n + 1) * 512],
                                  start=(k == 0), stop=(k == 1),
                                  waits=mmw if k == 0 else (), inc=(k == 1))
                    ev = s_op(lambda s, pbank=pbank, h=h, n=n:
                              s.activation(wt_o[(h % 2) * 64:(h % 2) * 64 + 64,
                                                (h // 2) * 1024 + n * 512:(h // 2) * 1024 + (n + 1) * 512],
                                           pbank[0:64, 0:512], AF.Copy),
                              waits=[mt], inc=True)
                    ev_tok[("oe", h, n)] = ev
                    ev_last = ev
                ev_tok[("o", h)] = ev_last

        S_.barrier()

    def phase_in(seq):
        with contextlib.ExitStack() as pp:
            io_tm = [pp.enter_context(nc.sbuf_tensor(f"in_tm{seq}_{i}", [128, 4 * 1024], F32))
                     for i in range(2)]
            cp_tok = {}
            for g4 in range(4):
                buf = io_tm[g4 % 2]
                sy_dma(buf[:, :].rearrange("p (t m) -> p t m", t=4),
                       x_in.ap()[seq * S + g4 * 512: seq * S + (g4 + 1) * 512, :]
                       .rearrange("(t p) m -> p t m", p=128),
                       waits=[cp_tok.get((g4 - 2, 31))])
                ld = S_.flush_dma("sync")
                for ts4 in range(4):
                    ts = g4 * 4 + ts4
                    sg = g4 * 4 + ts4
                    tr = None
                    for cc in range(CH):
                        tr = t_mm(psX[0:128, cc * 128:(cc + 1) * 128],
                                  buf[:, ts4 * 1024 + cc * 128: ts4 * 1024 + (cc + 1) * 128],
                                  identf[:, :], start=True, stop=True,
                                  waits=[ld, cp_tok.get(sg - 1)] if cc == 0 else (),
                                  inc=(cc == CH - 1))
                    cp_tok[sg] = v_op(
                        lambda v, ts=ts:
                        v.tensor_copy(bass.AP(x_fm, ts * 128, [[CH * S, 128], [S, CH], [1, 128]]),
                                      psX[:, :].rearrange("p (c n) -> p c n", c=CH)),
                        waits=[tr], inc=True)
                cp_tok[(g4, 31)] = cp_tok[g4 * 4 + 3]
        S_.barrier()

    def phase_ln(seq):
        k = u()
        with contextlib.ExitStack() as pp:
            P = pp.enter_context
            xb = P(nc.sbuf_tensor(f"lnxb{k}", [128, CH * 512], BF16))
            xq = P(nc.sbuf_tensor(f"lnxq{k}", [128, CH * 512], BF16))
            t_sb = P(nc.sbuf_tensor(f"lnt{k}", [128, 512], F32))
            mrow = P(nc.sbuf_tensor(f"lnm{k}", [1, 512], F32))
            qrow = P(nc.sbuf_tensor(f"lnq{k}", [1, 512], F32))
            arow = [P(nc.sbuf_tensor(f"lna{k}_{i}", [1, 512], F32)) for i in range(2)]
            brow = [P(nc.sbuf_tensor(f"lnb{k}_{i}", [1, 512], F32)) for i in range(2)]
            prev_apply = [None]
            stat_tok = {}
            row_tok = {}
            p1_tok = {}
            ap_tok = {}
            bc_pend = [None]

            def emit_bcast_apply():
                if bc_pend[0] is None:
                    return
                tt, c0, r6, r7, r1_ = bc_pend[0]
                bc_pend[0] = None
                tb = tt % 2
                ba = t_mm(ps4[0:128, 0:512], ones_row[0:1, 0:128], arow[tb][0:1, :],
                          start=True, stop=True, waits=[r6, r1_], inc=True)
                bb = t_mm(psx[0][0:128, 0:512], ones_row[0:1, 0:128], brow[tb][0:1, :],
                          start=True, stop=True, waits=[r7, prev_apply[0]], inc=True)
                xn3 = xn[:, :].rearrange("p (c n) -> p c n", c=CH)
                xf3 = x_fm[:, :].rearrange("p (c n) -> p c n", c=CH)
                a_bc = bass.AP(ps4, 0, [[512, 128], [0, CH], [1, 512]])
                b_bc = bass.AP(psX, 0, [[1024, 128], [0, CH], [1, 512]])
                p1 = v_op(lambda v, c0=c0, xf3=xf3, a_bc=a_bc:
                          v.tensor_mul(xn3[:, :, 1 + c0: 1 + c0 + 512],
                                       xf3[:, :, c0: c0 + 512], a_bc),
                          waits=[ba], inc=True)
                ap_t = v_op(lambda v, c0=c0, b_bc=b_bc:
                            v.tensor_sub(xn3[:, :, 1 + c0: 1 + c0 + 512],
                                         xn3[:, :, 1 + c0: 1 + c0 + 512], b_bc),
                            waits=[bb, p1], inc=True)
                p1_tok[tt] = p1
                ap_tok[tt] = ap_t
                prev_apply[0] = ap_t

            cast_map = {}

            def emit_casts(tt, gates=()):
                cast_t = []
                sq_t = []
                ct = v_op(lambda v, tt=tt:
                          v.tensor_copy(xb[:, :].rearrange("p (c n) -> p c n", c=CH),
                                        x_fm[:, :].rearrange("p (c n) -> p c n", c=CH)
                                        [:, :, tt * 512: tt * 512 + 512]),
                          waits=[stat_tok.get(tt - 1)] + list(gates), inc=True)
                sq = g_op(lambda g:
                          g.tensor_mul(xq[:, :], xb[:, :], xb[:, :]),
                          waits=[ct], inc=True)
                for cc in range(CH):
                    cast_t.append(ct)
                    sq_t.append(sq)
                cast_map[tt] = (cast_t, sq_t)

            emit_casts(0)
            for tt in range(NTT):
                c0 = tt * 512
                tb = tt % 2
                cast_t, sq_t = cast_map[tt]
                mt = None
                for cc in range(CH):
                    mt = t_mm(psx[1][0:1, 0:512], ln_ones[:, 0:1], xb[:, cc * 512:(cc + 1) * 512],
                              start=(cc == 0), stop=(cc == CH - 1),
                              waits=[cast_t[cc], row_tok.get(tt - 1)], inc=(cc == CH - 1))
                qt = None
                for cc in range(CH):
                    qt = t_mm(psx[1][32:33, 0:512], ln_ones[:, 0:1], xq[:, cc * 512:(cc + 1) * 512],
                              start=(cc == 0), stop=(cc == CH - 1),
                              waits=[sq_t[cc]], inc=(cc == CH - 1))
                stat_tok[tt] = qt
                if tt + 1 < NTT:
                    emit_casts(tt + 1)
                emit_bcast_apply()
                r1 = v_op(lambda v: v.tensor_copy(mrow[0:1, :], psx[1][0:1, 0:512]), waits=[mt, qt], inc=True)
                r2 = v_op(lambda v: v.tensor_copy(qrow[0:1, :], psx[1][32:33, 0:512]), waits=[qt], inc=True)
                row_tok[tt] = r2
                r3 = v_op(lambda v, tb=tb: v.tensor_mul(brow[tb][0:1, :], mrow[0:1, :], mrow[0:1, :]),
                          waits=[r1], inc=True)
                r4 = v_op(lambda v, tb=tb: v.tensor_sub(qrow[0:1, :], qrow[0:1, :], brow[tb][0:1, :]),
                          waits=[r2, r3], inc=True)
                r5 = s_op(lambda s, tb=tb: s.activation(brow[tb][0:1, :], qrow[0:1, :], AF.Ln,
                                                        bias=eps_col[0:1, 0:1]),
                          waits=[r4], inc=True)
                r6 = s_op(lambda s, tb=tb: s.activation(arow[tb][0:1, :], brow[tb][0:1, :], AF.Exp,
                                                        scale=-0.5),
                          waits=[r5], inc=True)
                r7 = v_op(lambda v, tb=tb: v.tensor_mul(brow[tb][0:1, :], mrow[0:1, :], arow[tb][0:1, :]),
                          waits=[r6], inc=True)
                bc_pend[0] = (tt, c0, r6, r7, r1)
            emit_bcast_apply()
        S_.barrier()

    def phase_ffn(seq):
        k = u()
        with contextlib.ExitStack() as pp:
            P = pp.enter_context
            d_t = P(nc.sbuf_tensor(f"ffnd{k}", [128, 512], BF16))
            xr = P(nc.sbuf_tensor(f"ffnxr{k}", [128, CH * 512], BF16))
            xk = P(nc.sbuf_tensor(f"ffnxk{k}", [128, CH * 512], BF16))
            r_sig = P(nc.sbuf_tensor(f"ffnrs{k}", [128, CH * 512], BF16))
            krelu = P(nc.sbuf_tensor(f"ffnkr{k}", [128, 512], BF16))
            k_act = P(nc.sbuf_tensor(f"ffnka{k}", [128, CH * 512], BF16))
            delta = P(nc.sbuf_tensor(f"ffndel{k}", [128, 512], F32))

            prev_de = {}
            prev_ka = {}
            prev_mm = {"r": None, "k": None, "v": None, "xk": None}
            for tt in range(NTT):
                c0 = tt * 512
                mix_r = []
                mix_k = []
                xn3 = xn[:, :].rearrange("p (c n) -> p c n", c=CH)
                dt = v_op(lambda v, c0=c0:
                          v.tensor_sub(k_act[:, :].rearrange("p (c n) -> p c n", c=CH),
                                       xn3[:, :, 1 + c0: 1 + c0 + 512],
                                       xn3[:, :, c0: c0 + 512]),
                          waits=[prev_mm["r"], prev_mm["k"], prev_mm["v"]], inc=True)
                for cc in range(CH):
                    prv = (cc * XNC + c0, cc * XNC + c0 + 512)
                    rt = v_op(lambda v, cc=cc, prv=prv:
                              v.scalar_tensor_tensor(xr[:, cc * 512:(cc + 1) * 512],
                                                     k_act[:, cc * 512:(cc + 1) * 512],
                                                     mu_r[:, cc:cc + 1], xn[:, prv[0]:prv[1]],
                                                     ALU.mult, ALU.add),
                              waits=[dt], inc=True)
                    kt_ = v_op(lambda v, cc=cc, prv=prv:
                               v.scalar_tensor_tensor(xk[:, cc * 512:(cc + 1) * 512],
                                                      k_act[:, cc * 512:(cc + 1) * 512],
                                                      mu_k[:, cc:cc + 1], xn[:, prv[0]:prv[1]],
                                                      ALU.mult, ALU.add),
                               waits=[dt], inc=True)
                    mix_r.append(rt)
                    mix_k.append(kt_)
                sig_t = []
                for oc in range(CH):
                    pbank = ps[oc % 2]
                    mt = None
                    for cc in range(CH):
                        mt = t_mm(pbank[0:128, 0:512],
                                  wt_r[:, cc * 1024 + oc * 128: cc * 1024 + (oc + 1) * 128],
                                  xr[:, cc * 512:(cc + 1) * 512],
                                  start=(cc == 0), stop=(cc == CH - 1),
                                  waits=([mix_r[-1],
                                          sig_t[oc - 2] if oc >= 2 else prev_de.get(oc + 6)]
                                         if cc == 0 else ()),
                                  inc=(cc == CH - 1))
                    st = s_op(lambda s, pbank=pbank, oc=oc:
                              s.activation(r_sig[:, oc * 512:(oc + 1) * 512], pbank[0:128, 0:512],
                                           AF.Sigmoid),
                              waits=[mt, prev_de.get(oc)], inc=True)
                    sig_t.append(st)
                ka_t = []
                for oc in range(CH):
                    pbank = ps[oc % 2]
                    mt = None
                    for cc in range(CH):
                        mt = t_mm(pbank[0:128, 0:512],
                                  wt_k[:, cc * 1024 + oc * 128: cc * 1024 + (oc + 1) * 128],
                                  xk[:, cc * 512:(cc + 1) * 512],
                                  start=(cc == 0), stop=(cc == CH - 1),
                                  waits=([mix_k[-1],
                                          ka_t[oc - 2] if oc >= 2 else sig_t[oc + 6]]
                                         if cc == 0 else ()),
                                  inc=(cc == CH - 1))
                    rl = s_op(lambda s, pbank=pbank:
                              s.activation(krelu[:, :], pbank[0:128, 0:512], AF.Relu),
                              waits=[mt, ka_t[oc - 1] if oc >= 1 else prev_ka.get(7)], inc=True)
                    ka = v_op(lambda v, pbank=pbank, oc=oc:
                              v.tensor_mul(k_act[:, oc * 512:(oc + 1) * 512], pbank[0:128, 0:512],
                                           krelu[:, :]),
                              waits=[rl], inc=True)
                    ka_t.append(ka)
                add_t = []
                for oc in range(CH):
                    pbank = ps[oc % 2]
                    mt = None
                    for cc in range(CH):
                        mt = t_mm(pbank[0:128, 0:512],
                                  wt_v[:, cc * 1024 + oc * 128: cc * 1024 + (oc + 1) * 128],
                                  k_act[:, cc * 512:(cc + 1) * 512],
                                  start=(cc == 0), stop=(cc == CH - 1),
                                  waits=([ka_t[CH - 1], sig_t[CH - 1],
                                          add_t[oc - 2] if oc >= 2 else None]
                                         if cc == 0 else ()),
                                  inc=(cc == CH - 1))
                    de = v_op(lambda v, pbank=pbank, oc=oc:
                              v.scalar_tensor_tensor(delta[:, :], pbank[0:128, 0:512], 0.5,
                                                     r_sig[:, oc * 512:(oc + 1) * 512],
                                                     ALU.mult, ALU.mult),
                              waits=[mt], inc=True)
                    at = v_op(lambda v, oc=oc, c0=c0:
                              v.tensor_add(x_fm[:, oc * S + c0: oc * S + c0 + 512],
                                           x_fm[:, oc * S + c0: oc * S + c0 + 512], delta[:, :]),
                              waits=[de], inc=True)
                    add_t.append(at)
                    prev_de[oc] = de
                for oc in range(CH):
                    prev_ka[oc] = ka_t[oc]
                prev_mm["r"] = sig_t[CH - 1]
                prev_mm["k"] = ka_t[CH - 1]
                prev_mm["v"] = mt
        S_.barrier()

    def phase_attn(seq):
        with contextlib.ExitStack() as pp:
            P = pp.enter_context
            latent = P(nc.sbuf_tensor(f"lat{seq}", [128, S], BF16))
            lat_tm = P(nc.sbuf_tensor(f"lattm{seq}", [128, NKT * 128], BF16))
            qd = [P(nc.sbuf_tensor(f"qd{seq}_{i}", [128, 512], BF16)) for i in range(2)]
            expT = [P(nc.sbuf_tensor(f"expT{seq}_{i}", [128, 1024], BF16)) for i in range(2)]
            u_fm = P(nc.sbuf_tensor(f"ufm{seq}", [128, 2 * 4 * 512], BF16))
            row_a = P(nc.sbuf_tensor(f"rowa{seq}", [1, 512], F32))
            row_b = P(nc.sbuf_tensor(f"rowb{seq}", [1, 512], F32))
            row_c = P(nc.sbuf_tensor(f"rowc{seq}", [1, 512], F32))
            rb = P(nc.sbuf_tensor(f"rb{seq}", [64, 512], F32))

            lat_ev = []
            for qg in range(NQG):
                pbank = ps[qg % 2]
                mt = None
                for cc in range(CH):
                    mt = t_mm(pbank[0:128, 0:512],
                              wt_kv[:, cc * 128:(cc + 1) * 128],
                              xn[:, cc * XNC + 1 + qg * 512: cc * XNC + 1 + qg * 512 + 512],
                              start=(cc == 0), stop=(cc == CH - 1),
                              waits=([lat_ev[qg - 2] if qg >= 2 else None] if cc == 0 else ()),
                              inc=(cc == CH - 1))
                ev = s_op(lambda s, pbank=pbank, qg=qg:
                          s.activation(latent[:, qg * 512:(qg + 1) * 512], pbank[0:128, 0:512],
                                       AF.Copy),
                          waits=[mt], inc=True)
                lat_ev.append(ev)
            zpad_t = g_op(lambda g: g.memset(lat_tm[:, :], 0.0), inc=True)
            tm_cp = {}
            for kt in range(NKT):
                S_.emit("sync",
                        lambda sy, kt=kt: sy.dma_start_transpose(
                            lat_tm[:, kt * 128: kt * 128 + 64],
                            latent[0:64, kt * 128:(kt + 1) * 128]),
                        waits=[lat_ev[kt // 4], zpad_t], dma=True)
                tm_cp[kt] = S_.flush_dma("sync") if kt == NKT - 1 else None
            ones_t = g_op(lambda g: g.memset(
                lat_tm[:, :].rearrange("p (kt c) -> p kt c", kt=NKT)[:, :, 64:65], 1.0),
                waits=[zpad_t], inc=True)

            u_ev = {}
            add_t = {}
            qd_ev = {}
            prev_c5 = [None]
            norm_pend = [None]

            def emit_norm():
                if norm_pend[0] is None:
                    return
                nqg, nh, nb, num = norm_pend[0]
                norm_pend[0] = None
                c1 = v_op(lambda v, nb=nb: v.tensor_copy(row_a[0:1, :], psx[nb][64:65, 0:512]),
                          waits=[num], inc=True)
                c3 = v_op(lambda v: v.reciprocal(row_b[0:1, :], row_a[0:1, :]),
                          waits=[c1], inc=True)
                c4 = t_mm(ps4[0:64, 0:512], ones_row[0:1, 0:64], row_b[0:1, :],
                          start=True, stop=True, waits=[c3, prev_c5[0]], inc=True)
                c5 = v_op(lambda v: v.tensor_copy(rb[0:64, :], ps4[0:64, 0:512]),
                          waits=[c4], inc=True)
                prev_c5[0] = c5
                ub = (nqg % 2) * 2048
                ue = v_op(lambda v, nb=nb, nh=nh, ub=ub:
                          v.tensor_mul(u_fm[(nh % 2) * 64:(nh % 2) * 64 + 64,
                                            ub + (nh // 2) * 512: ub + (nh // 2) * 512 + 512],
                                       psx[nb][0:64, 0:512], rb[0:64, :]),
                          waits=[c5], inc=True)
                u_ev[(nqg, nh)] = ue

            qt_mt = {}
            bank_free = [None]

            def emit_qt(qg, h):
                q0 = qg * 512
                mt = None
                for cc in range(CH):
                    mt = t_mm(psQ[0:128, 0:512],
                              wt_q[:, cc * 1024 + h * 128: cc * 1024 + (h + 1) * 128],
                              xn[:, cc * XNC + 1 + q0: cc * XNC + 1 + q0 + 512],
                              start=(cc == 0), stop=(cc == CH - 1),
                              waits=([bank_free[0]] if cc == 0 else ()),
                              inc=(cc == CH - 1))
                qt_mt[(qg, h)] = mt

            def emit_qe(qg, h):
                qe = v_op(lambda v, h=h: v.tensor_copy(qd[h % 2][:, :], psQ[0:128, 0:512]),
                          waits=[qt_mt[(qg, h)]], inc=True)
                qd_ev[(qg, h)] = qe
                bank_free[0] = qe

            def emit_out_grp(oqg, oc):
                oq0 = oqg * 512
                ub = (oqg % 2) * 2048
                mt = None
                for uc in range(4):
                    mt = t_mm(psQ[0:128, 0:512],
                              wt_o[:, uc * 1024 + oc * 128: uc * 1024 + (oc + 1) * 128],
                              u_fm[:, ub + uc * 512: ub + (uc + 1) * 512],
                              start=(uc == 0), stop=(uc == 3),
                              waits=([u_ev[(oqg, hh)] for hh in range(H)] +
                                     [bank_free[0]] if uc == 0 else ()),
                              inc=(uc == 3))
                at = v_op(lambda v, oc=oc, oq0=oq0:
                          v.tensor_add(x_fm[:, oc * S + oq0: oc * S + oq0 + 512],
                                       x_fm[:, oc * S + oq0: oc * S + oq0 + 512],
                                       psQ[0:128, 0:512]),
                          waits=[mt], inc=True)
                add_t[(oqg, oc)] = at
                bank_free[0] = at

            for qg in range(NQG):
                q0 = qg * 512
                for h in range(H):
                    b = h % 2
                    if (qg, h) not in qt_mt:
                        emit_qt(qg, h)
                    if (qg, h) not in qd_ev:
                        emit_qe(qg, h)
                    qe = qd_ev[(qg, h)]
                    exp_t = {}
                    NP = NKT // 2

                    def emit_u(j, qg=qg, h=h, b=b, exp_t=exp_t):
                        um = t_mm(psx[b][0:128, 0:512], lat_tm[:, j * 128: j * 128 + 128],
                                  expT[(j // 2) % 2][:, (j % 2) * 512:(j % 2) * 512 + 512],
                                  start=(j == 0), stop=(j == NKT - 1),
                                  waits=[exp_t[j // 2]] + ([u_ev.get((qg, h - 2)),
                                                            u_ev.get((qg - 1, h + 6)) if h < 2 else None]
                                                           if j == 0 else []),
                                  inc=True)
                        exp_t[("u", j)] = um
                        return um

                    for p in range(NP):
                        pt = psA if p % 2 == 0 else psB
                        st = None
                        for half in range(2):
                            kt = 2 * p + half
                            st = t_mm(pt[0:128, half * 512: half * 512 + 512],
                                      latent[:, kt * 128:(kt + 1) * 128], qd[b][:, :],
                                      start=True, stop=True,
                                      waits=[qe,
                                             exp_t.get(p - 2) if half == 0 else None,
                                             (tm_cp[NKT - 1] if (qg == 0 and h == 0 and p == 0) else None),
                                             (ones_t if (qg == 0 and h == 0 and p == 0) else None)],
                                      inc=True)
                        et = s_op(lambda s, pt=pt, p=p:
                                  s.activation(expT[p % 2][:, 0:1024], pt[0:128, 0:1024],
                                               AF.Exp, bias=ebias_col[:, 0:1]),
                                  waits=[st, exp_t.get(("u", 2 * (p - 2) + 1))], inc=True)
                        exp_t[p] = et
                        if p >= 1:
                            emit_u(2 * (p - 1))
                            emit_u(2 * (p - 1) + 1)
                        nh = h + 1
                        nqg2 = qg if nh < H else qg + 1
                        nh = nh if nh < H else 0
                        if p == 1 and nqg2 < NQG:
                            emit_qt(nqg2, nh)
                        if p == 2:
                            emit_norm()
                        if p == 3 and nqg2 < NQG:
                            emit_qe(nqg2, nh)
                        if p == 4 and qg >= 1:
                            emit_out_grp(qg - 1, h)
                    emit_u(NKT - 2)
                    um = emit_u(NKT - 1)
                    norm_pend[0] = (qg, h, b, um)
            emit_norm()
            for oc in range(CH):
                emit_out_grp(NQG - 1, oc)
        S_.barrier()

    def phase_out(seq):
        with contextlib.ExitStack() as pp:
            io_tm = [pp.enter_context(nc.sbuf_tensor(f"out_tm{seq}_{i}", [128, 4 * 1024], F32))
                     for i in range(2)]
            st_tok = {}
            cp_tok = {}
            for g4 in range(4):
                buf = io_tm[g4 % 2]
                cps = []
                for ts4 in range(4):
                    ts = g4 * 4 + ts4
                    sg = g4 * 4 + ts4
                    pt = psA if sg % 2 == 0 else psB
                    tr = None
                    for cc in range(CH):
                        tr = t_mm(pt[0:128, cc * 128:(cc + 1) * 128],
                                  x_fm[:, cc * S + ts * 128: cc * S + (ts + 1) * 128],
                                  identf[:, :], start=True, stop=True,
                                  waits=[cp_tok.get(sg - 2)] if cc == 0 else (),
                                  inc=(cc == CH - 1))
                    cp = v_op(lambda v, pt=pt, ts4=ts4, buf=buf:
                              v.tensor_copy(buf[:, ts4 * 1024: (ts4 + 1) * 1024], pt[0:128, 0:1024]),
                              waits=[tr, st_tok.get(g4 - 2)], inc=True)
                    cp_tok[sg] = cp
                    cps.append(cp)
                sy_dma(
                    y_out.ap()[seq * S + g4 * 512: seq * S + (g4 + 1) * 512, :]
                    .rearrange("(t p) m -> p t m", p=128),
                    buf[:, :].rearrange("p (t m) -> p t m", t=4),
                    waits=cps)
                st_tok[g4] = S_.flush_dma("sync")
        S_.barrier()

    g_op(lambda g: g.memset(
        xn[:, :].rearrange("p (c t) -> p c t", c=CH)[:, :, 0:1], 0.0), inc=True)

    def dbg_dump(t, sb):
        S_.emit("sync", lambda sy, t=t, sb=sb: sy.dma_start(t.ap()[:, :], sb[:, :]), dma=True)
        S_.flush_dma("sync")
        S_.barrier()

    for seq in range(NSEQ):
        phase_in(seq)
        phase_ln(seq)
        if DEBUG and seq == 0:
            dbg_dump(dbg_xn, xn)
        phase_ffn(seq)
        if DEBUG and seq == 0:
            dbg_dump(dbg_x1, x_fm)
        phase_ln(seq)
        phase_attn(seq)
        if DEBUG and seq == 0:
            dbg_dump(dbg_x2, x_fm)
        phase_ln(seq)
        phase_ffn(seq)
        phase_out(seq)
    if DEBUG:
        dbg_dump(dbg_wtq, wt_q)
        dbg_dump(dbg_wto, wt_o)
        dbg_dump(dbg_wtr, wt_r)
        dbg_dump(dbg_wtkv, wt_kv)

    S_.flush_dma("sync")
    S_.flush_dma("gpsimd")

    with nc.Block() as blk:
        @blk.tensor
        def _(t):
            for f in S_.ops["tensor"]:
                f(t)

        @blk.vector
        def _(v):
            for f in S_.ops["vector"]:
                f(v)

        @blk.scalar
        def _(s):
            for f in S_.ops["scalar"]:
                f(s)

        @blk.gpsimd
        def _(g):
            for f in S_.ops["gpsimd"]:
                f(g)

        @blk.sync
        def _(sy):
            for f in S_.ops["sync"]:
                f(sy)

    es.close()
    return nc


def kernel(L0_attn_x, L0_attn_Wq, L0_attn_Wkv_down, L0_attn_Wk_up, L0_attn_Wv_up,
           L0_attn_Wo, L0_ffn_W_r, L0_ffn_W_k, L0_ffn_W_v, L0_ffn_mu_r, L0_ffn_mu_k):
    x = np.ascontiguousarray(np.asarray(L0_attn_x, dtype=np.float32))
    shared = {
        "wq": np.ascontiguousarray(np.asarray(L0_attn_Wq, np.float32)),
        "wkup": np.ascontiguousarray(np.asarray(L0_attn_Wk_up, np.float32)),
        "wvup": np.ascontiguousarray(np.asarray(L0_attn_Wv_up, np.float32)),
        "wot": np.ascontiguousarray(np.asarray(L0_attn_Wo, np.float32).T),
        "wkvt": np.ascontiguousarray(np.asarray(L0_attn_Wkv_down, np.float32).T),
        "wrt": np.ascontiguousarray(np.asarray(L0_ffn_W_r, np.float32).T),
        "wkt": np.ascontiguousarray(np.asarray(L0_ffn_W_k, np.float32).T),
        "wvt": np.ascontiguousarray(np.asarray(L0_ffn_W_v, np.float32).T),
        "mur": np.ascontiguousarray(np.asarray(L0_ffn_mu_r, np.float32).reshape(8, 128).T),
        "muk": np.ascontiguousarray(np.asarray(L0_ffn_mu_k, np.float32).reshape(8, 128).T),
        "ident": np.eye(128, dtype=np.float32),
    }
    in_maps = []
    for c in range(NCORES):
        shard = np.ascontiguousarray(x[c * NSEQ:(c + 1) * NSEQ].reshape(NSEQ * S, C))
        in_maps.append({"x": shard, **shared})
    nc = build_nc()
    res = run_bass_kernel_spmd(nc, in_maps, core_ids=list(range(NCORES)))
    out = np.empty((16, S, C), dtype=np.float32)
    for c in range(NCORES):
        out[c * NSEQ:(c + 1) * NSEQ] = res.results[c]["y"].reshape(NSEQ, S, C)
    return out
